# version 01
# baseline (speedup 1.0000x reference)
import numpy as np
import ml_dtypes

import concourse.bass as bass
import concourse.mybir as mybir
from concourse.bass_utils import run_bass_kernel_spmd

F32 = mybir.dt.float32
BF16 = mybir.dt.bfloat16
AF = mybir.ActivationFunctionType
ALU = mybir.AluOpType
AX = mybir.AxisListType.X
BF = ml_dtypes.bfloat16

E = 2048
T = 1024
NIC = 16
NOC = 16
H = 16
D = 128
SP = 8
NG = T // SP
HID = 8192
NHC = 64
HB = 8
NBLK = NHC // HB
INV_SQRT_D = float(1.0 / np.sqrt(128.0))
LN_EPS = 1e-5

OX = 0
OQ = 16384
OK = 32768
OV = 49152
OH = 65536
HEAP_N = 98304

ORC = OH
ORS = OH + 1024
OSC = OH + 2048
OSTG = OH + 7168
ORING = OH + 11264
OEYE = OH + 17408
OMSK = OH + 17536
AKGC = OH + 17664
AVGC = OH + 18688
AVT = OH + 19712
AEXP = OH + 20736
ASCR = OH + 20992
AZ = OH + 21248

PAIR_SWAP = [x for j in range(16) for x in (2 * j + 1, 2 * j)]


def _tile_lhsT(w, O, I):
    oc, ic = O // 128, I // 128
    return np.ascontiguousarray(
        w.reshape(oc, 128, ic, 128).transpose(0, 3, 2, 1).reshape(oc, 128, I))


def _tile_lhsT_by_in(w, O, I):
    oc, ic = O // 128, I // 128
    return np.ascontiguousarray(
        w.reshape(oc, 128, ic, 128).transpose(2, 3, 0, 1).reshape(ic, 128, O))


def _rope_tables_for(pos):
    inv = 1.0 / (10000.0 ** (np.arange(0, D, 2, dtype=np.float64) / D))
    fr = np.outer(inv, pos.astype(np.float64))
    cos = np.cos(fr)
    sin = np.sin(fr)
    ct = np.repeat(cos, 2, axis=0)
    st = np.empty((D, len(pos)), dtype=np.float64)
    st[0::2] = -sin
    st[1::2] = sin
    return ct.astype(np.float32), st.astype(np.float32)


class Sched:
    def __init__(self):
        self.ops = {k: [] for k in ("tensor", "vector", "scalar", "sync", "gpsimd")}
        self.cnt = {}
        self.mark = {}
        self.own = {"sT": "tensor", "sV": "vector", "sS": "scalar"}

    def inc(self, s, n=1):
        self.cnt[s] = self.cnt.get(s, 0) + n
        return self.cnt[s]

    def val(self, s):
        return self.cnt.get(s, 0)

    def add(self, eng, fn, waits=(), inc=None):
        real = []
        for w in waits:
            if w is None:
                continue
            sname, v = w
            if v <= 0 or self.own.get(sname) == eng:
                continue
            key = (eng, sname)
            if self.mark.get(key, -1) >= v:
                continue
            self.mark[key] = v
            real.append((sname, v))
        newv = self.inc(*inc) if inc is not None else None
        self.ops[eng].append((fn, tuple(real), inc))
        return newv


def build_nc(debug=(), stop_after="full"):
    nc = bass.Bass("TRN2", target_bir_lowering=False, detect_race_conditions=False)

    xQT = nc.dram_tensor("xQT", [E, T], F32, kind="ExternalInput").ap()
    xRT = nc.dram_tensor("xRT", [E, T], F32, kind="ExternalInput").ap()
    w_in = {}
    for nm, n_oc in (("wq", NOC), ("wk", NOC), ("wv", NOC), ("wo", NOC),
                     ("wg", NHC), ("wu", NHC), ("wd", NHC)):
        w_in[nm] = nc.dram_tensor(nm, [n_oc, 128, E], F32, kind="ExternalInput").ap()
    bias_in = nc.dram_tensor("bias", [128, 64], F32, kind="ExternalInput").ap()
    eye_in = nc.dram_tensor("eye", [128, 128], BF16, kind="ExternalInput").ap()
    mask_in = nc.dram_tensor("mask", [128, 128], BF16, kind="ExternalInput").ap()
    ropec_in = nc.dram_tensor("ropec", [128, T], F32, kind="ExternalInput").ap()
    ropes_in = nc.dram_tensor("ropes", [128, T], F32, kind="ExternalInput").ap()
    outT = nc.dram_tensor("outT", [E, T], F32, kind="ExternalOutput").ap()

    dbg_aps = {}
    for name, shape, dt in (("dbg_qkv", [128, 3 * NIC * T], BF16),
                            ("dbg_ao", [128, NIC * T], BF16),
                            ("dbg_h1", [128, NIC * T], F32),
                            ("dbg_ln1", [128, NIC * T], F32)):
        if name in debug:
            dbg_aps[name] = nc.dram_tensor(name, shape, dt, kind="ExternalOutput").ap()

    ctx = []

    def mk(fn, *a, **k):
        t = fn(*a, **k)
        ctx.append(t)
        return t.__enter__()

    heap_t = mk(nc.sbuf_tensor, "heap", [128, HEAP_N], BF16)
    bias_t = mk(nc.sbuf_tensor, "biasb", [128, 64], F32)
    ones1_t = mk(nc.sbuf_tensor, "ones1", [128, 1], BF16)
    onesr_t = mk(nc.sbuf_tensor, "onesr", [1, 128], F32)
    one11_t = mk(nc.sbuf_tensor, "one11", [1, 1], F32)
    rrow_t = mk(nc.sbuf_tensor, "rrow", [1, 256], F32)
    rcol_t = mk(nc.sbuf_tensor, "rcol", [128, 2], F32)
    vscr_t = mk(nc.sbuf_tensor, "vscr", [128, 512], F32)
    stats_t = mk(nc.sbuf_tensor, "stats", [1, 2112], F32)

    P0 = mk(nc.psum_tensor, "P0", [128, 1024], F32)
    P1 = mk(nc.psum_tensor, "P1", [128, 1024], F32)
    P2 = mk(nc.psum_tensor, "P2", [128, 1024], F32)
    P3 = mk(nc.psum_tensor, "P3", [128, 512], F32)
    P4 = mk(nc.psum_tensor, "P4", [128, 512], F32)
    PBIG = [P0, P1, P2]

    sems = {}
    for nm in ("sT", "sS", "sV", "sX", "sCN", "sST", "sDBG",
               "sSG0", "sSG1", "sSG2", "sSG3"):
        sems[nm] = mk(nc.semaphore, nm)

    B = Sched()

    def heap(lo, n):
        return heap_t[:, lo:lo + n]

    def heapf(lo, n):
        return heap_t[:, lo:lo + 2 * n].bitcast(F32)

    def dma(eng, out, in_, waits=(), sem="sST"):
        def f(e, out=out, in_=in_):
            return e.dma_start(out=out, in_=in_)
        return B.add(eng, f, waits=waits, inc=(sem, 16))

    def mm(out, lhsT, rhs, start, stop, waits=(), inc=False):
        def f(e, out=out, lhsT=lhsT, rhs=rhs, start=start, stop=stop):
            return e.matmul(out, lhsT, rhs, start=start, stop=stop)
        return B.add("tensor", f, waits=waits, inc=("sT", 1) if inc else None)

    def act(out, in_, func, bias=None, scale=None, waits=()):
        def f(e, out=out, in_=in_, func=func, bias=bias, scale=scale):
            kw = {}
            if bias is not None:
                kw["bias"] = bias
            if scale is not None:
                kw["scale"] = scale
            return e.activation(out, in_, func, **kw)
        return B.add("scalar", f, waits=waits, inc=("sS", 1))

    def vop(kind, *args, waits=()):
        def f(e, kind=kind, args=args):
            return getattr(e, kind)(*args)
        return B.add("vector", f, waits=waits, inc=("sV", 1))

    xf = heapf(OH, NIC * T)
    dma("sync", xf.rearrange("p (ic t) -> p ic t", t=T),
        xQT.rearrange("(ic p) t -> p ic t", p=128), sem="sX")
    xq_loaded = ("sX", B.val("sX"))
    dma("sync", bias_t[:], bias_in, sem="sCN")
    vop("memset", ones1_t[:], 1.0)
    vop("memset", onesr_t[:], 1.0)
    vop("memset", one11_t[:], 1.0)
    memset_done = vop("memset", stats_t[0:1, 2088:2089], LN_EPS)
    eps_ap = stats_t[0:1, 2088:2089]

    xq_done = 0
    for ic in range(NIC):
        xq_done = act(heap(OX + ic * T, T), xf[:, ic * T:(ic + 1) * T], AF.Identity,
                      waits=[xq_loaded])
    xcast_done = ("sS", xq_done)
    dma("sync", heap(OEYE, 128), eye_in, waits=[xcast_done], sem="sCN")
    dma("sync", heap(OMSK, 128), mask_in, sem="sCN")
    consts_loaded = ("sCN", B.val("sCN"))
    az_done = vop("memset", heap(AZ, 128), 0.0, waits=[xcast_done])

    stg_state = {"idx": 1, "rel": [xcast_done, xcast_done], "base": OSTG}

    def load_cast(src_ap, dst_ap, eng="scalar", extra=()):
        i = stg_state["idx"]
        stg_state["idx"] += 1
        sslot = i % 2
        sgname = f"sSG{i % 4}"
        n = src_ap.shape[-1]
        stg = heapf(stg_state["base"] + sslot * 2048, n)
        dvv = dma("sync", stg, src_ap,
                  waits=[stg_state["rel"][sslot]] + list(extra), sem=sgname)
        w = [(sgname, dvv)]
        if eng == "scalar":
            cv = act(dst_ap, stg, AF.Identity, waits=w)
            mkr = ("sS", cv)
        else:
            cv = vop("tensor_copy", dst_ap, stg, waits=w)
            mkr = ("sV", cv)
        stg_state["rel"][sslot] = mkr
        return mkr

    def load_cast_slab(src_slab, dst_slab, dst_rel, eng):
        mkr = None
        for hh in range(2):
            mkr = load_cast(src_slab[:, hh * 1024:(hh + 1) * 1024],
                            dst_slab[:, hh * 1024:(hh + 1) * 1024], eng,
                            extra=[dst_rel] if (hh == 0 and dst_rel) else ())
        return mkr

    wring = {"rel": [None] * 3, "idx": 0, "base": ORING}

    def stream_weight(src_ap, eng):
        j = wring["idx"]
        wring["idx"] += 1
        slot = j % 3
        slab = heap(wring["base"] + slot * E, E)
        mkr = load_cast_slab(src_ap, slab, wring["rel"][slot], eng)
        return slab, mkr, slot

    ct_done = load_cast(ropec_in, heap(ORC, T), "scalar", extra=[xcast_done])
    st_done = load_cast(ropes_in, heap(ORS, T), "scalar")

    prel = {id(p): None for p in (P0, P1, P2, P3, P4)}

    def release(P, sem, val):
        prel[id(P)] = (sem, val)

    rot = {"i": 0}

    def pick_big():
        P = PBIG[rot["i"] % 3]
        rot["i"] += 1
        return P

    qtmp_pp = [heap(OSC, T), heap(OSC + 1024, T)]
    qs_scr = heap(OSC + 2048, T)
    t1_scr = heap(OSC + 3072, T)
    t2_scr = heap(OSC + 4096, T)
    rope_pp_rel = [None, None]
    proj_done = {}
    for which, wname, bcol0, obase in (("q", "wq", 0, OQ), ("k", "wk", 16, OK),
                                       ("v", "wv", 32, OV)):
        outs = []
        for oc in range(NOC):
            slab, wmk, slot = stream_weight(w_in[wname][oc, :, :], "vector")
            P = pick_big()
            for ic in range(NIC):
                for half in range(2):
                    mm(P[:, half * 512:(half + 1) * 512],
                       slab[:, ic * 128:(ic + 1) * 128],
                       heap(OX + ic * T + half * 512, 512),
                       start=(ic == 0), stop=(ic == NIC - 1),
                       waits=[wmk, prel[id(P)], xcast_done, consts_loaded]
                       if (ic == 0 and half == 0) else (),
                       inc=(ic == NIC - 1 and half == 1))
            mmv = B.val("sT")
            wring["rel"][slot] = ("sT", mmv)
            if which == "v":
                sv = act(heap(obase + oc * T, T), P[:], AF.Identity,
                         bias=bias_t[:, bcol0 + oc:bcol0 + oc + 1],
                         waits=[("sT", mmv)])
                release(P, "sS", sv)
                outs.append(("sS", sv))
            else:
                pp = oc % 2
                sv = act(qtmp_pp[pp], P[:], AF.Identity,
                         bias=bias_t[:, bcol0 + oc:bcol0 + oc + 1],
                         waits=[("sT", mmv), rope_pp_rel[pp]])
                release(P, "sS", sv)
                vop("stream_shuffle", qs_scr, qtmp_pp[pp], PAIR_SWAP,
                    waits=[("sS", sv), ct_done, st_done])
                vop("tensor_tensor", t1_scr, qtmp_pp[pp], heap(ORC, T), ALU.mult)
                vop("tensor_tensor", t2_scr, qs_scr, heap(ORS, T), ALU.mult)
                rv = vop("tensor_tensor", heap(obase + oc * T, T),
                         t1_scr, t2_scr, ALU.add)
                rope_pp_rel[pp] = ("sV", rv)
                outs.append(("sV", rv))
        proj_done[which] = outs[-1]

    if stop_after == "qkv":
        if "dbg_qkv" in dbg_aps:
            d = dbg_aps["dbg_qkv"]
            dma("sync", d[:, 0:NIC * T], heap(OQ, NIC * T),
                waits=[proj_done["q"], proj_done["k"], proj_done["v"]], sem="sDBG")
            dma("sync", d[:, NIC * T:2 * NIC * T], heap(OK, NIC * T), sem="sDBG")
            dma("sync", d[:, 2 * NIC * T:3 * NIC * T], heap(OV, NIC * T),
                sem="sDBG")
        return _finish(nc, B, sems)

    kg_all = heap(OK, NIC * T).rearrange("p (t s) -> p s t", s=T)
    qg_all = heap(OQ, NIC * T).rearrange("p (t s) -> p s t", s=T)
    vg_all = heap(OV, NIC * T).rearrange("p (t s) -> p s t", s=T)
    x_dead = ("sT", B.val("sT"))
    akgc_rel = [None, None]
    avgc_rel = [None, None]
    avt_rel = [None, None]
    p0h_rel = {0: prel[id(P0)], 1: prel[id(P0)]}
    p1h_rel = {0: prel[id(P1)], 1: prel[id(P1)]}
    p2h_rel = {0: prel[id(P2)], 1: prel[id(P2)]}
    kgc_act = [None, None]
    vgc_act = [None, None]
    avt_act = [None, None]
    b_rel = [None, None]
    sums_rel = [None, None]
    attv_rel = [None, None]
    aot_rel = [None, None]
    exp_free = [None, None]
    rrow_free = [None, None]
    rcol_free = [None, None]
    ascr_free = [None, None]
    exp_act = [None, None]
    ao_done = None

    for g in range(NG):
        p = g % 2
        bi = g // 4
        bp = bi % 2
        if g % 4 == 0:
            s0 = bi * 32
            mm(P0[:, 512 * bp:512 * bp + 512], heap(OEYE, 128),
               kg_all[:, s0:s0 + 32, :], start=True, stop=True,
               waits=[p0h_rel[bp], consts_loaded], inc=True)
            kv = B.val("sT")
            cv_ = act(heap(AKGC + 512 * bp, 512), P0[:, 512 * bp:512 * bp + 512],
                      AF.Identity, waits=[("sT", kv), akgc_rel[bp]])
            kgc_act[bp] = ("sS", cv_)
            p0h_rel[bp] = ("sS", cv_)
            mm(P1[:, 512 * bp:512 * bp + 512], heap(OEYE, 128),
               vg_all[:, s0:s0 + 32, :], start=True, stop=True,
               waits=[p1h_rel[bp]], inc=True)
            vv_ = B.val("sT")
            cv_ = act(heap(AVGC + 512 * bp, 512), P1[:, 512 * bp:512 * bp + 512],
                      AF.Identity, waits=[("sT", vv_), avgc_rel[bp]])
            vgc_act[bp] = ("sS", cv_)
            p1h_rel[bp] = ("sS", cv_)
            for i in range(4):
                mm(P2[:, 512 * bp + 128 * i:512 * bp + 128 * (i + 1)],
                   heap(AVGC + 512 * bp + 128 * i, 128), heap(OEYE, 128),
                   start=True, stop=True,
                   waits=[vgc_act[bp], p2h_rel[bp]] if i == 0 else (),
                   inc=(i == 3))
            tv_ = B.val("sT")
            cv_ = act(heap(AVT + 512 * bp, 512), P2[:, 512 * bp:512 * bp + 512],
                      AF.Identity, waits=[("sT", tv_), avt_rel[bp]])
            avt_act[bp] = ("sS", cv_)
            p2h_rel[bp] = ("sS", cv_)
            akgc_rel[bp] = None
            avgc_rel[bp] = None
            avt_rel[bp] = None
        BNK = P3 if p == 0 else P4
        mm(BNK[:, 0:128], heap(AKGC + 512 * bp + 128 * (g % 4), 128),
           qg_all[:, SP * g:SP * (g + 1), :], start=True, stop=True,
           waits=[kgc_act[bp], b_rel[p], proj_done["q"]], inc=True)
        sc_mm = B.val("sT")
        if g % 4 == 3:
            akgc_rel[bp] = ("sT", sc_mm)
        ev = act(heap(AEXP + 128 * p, 128), BNK[:, 0:128], AF.Exp,
                 scale=INV_SQRT_D, waits=[("sT", sc_mm), exp_free[p]])
        exp_act[p] = ("sS", ev)
        mv = vop("tensor_tensor", heap(AEXP + 128 * p, 128),
                 heap(AEXP + 128 * p, 128), heap(OMSK, 128), ALU.mult,
                 waits=[("sS", ev), consts_loaded])
        mm(BNK[0:1, 128:256], ones1_t[:], heap(AEXP + 128 * p, 128),
           start=True, stop=True,
           waits=[("sV", mv), sums_rel[p], ("sV", memset_done)], inc=True)
        sm_mm = B.val("sT")
        rv = vop("reciprocal", rrow_t[0:1, 128 * p:128 * (p + 1)],
                 BNK[0:1, 128:256], waits=[("sT", sm_mm), rrow_free[p]])
        sums_rel[p] = ("sV", rv)
        mm(BNK[:, 0:1], rrow_t[0:1, 128 * p:128 * (p + 1)], one11_t[:],
           start=True, stop=True,
           waits=[("sV", rv), ("sS", ev), ("sV", memset_done)], inc=True)
        rc_mm = B.val("sT")
        rrow_free[p] = ("sT", rc_mm)
        rcv = act(rcol_t[:, p:p + 1], BNK[:, 0:1], AF.Identity,
                  waits=[("sT", rc_mm), rcol_free[p]])
        mm(BNK[:, 256:384], heap(AEXP + 128 * p, 128),
           heap(AVT + 512 * bp + 128 * (g % 4), 128), start=True, stop=True,
           waits=[avt_act[bp], attv_rel[p]], inc=True)
        av_mm = B.val("sT")
        exp_free[p] = ("sT", av_mm)
        if g % 4 == 3:
            avt_rel[bp] = ("sT", av_mm)
            avgc_rel[bp] = avt_act[bp]
        nv = vop("scalar_tensor_tensor", heap(ASCR + 128 * p, 128),
                 BNK[:, 256:384], rcol_t[:, p:p + 1], heap(AZ, 128),
                 ALU.mult, ALU.add,
                 waits=[("sT", av_mm), ("sS", rcv), ascr_free[p]])
        attv_rel[p] = ("sV", nv)
        rcol_free[p] = ("sV", nv)
        mm(BNK[:, 384:512], heap(ASCR + 128 * p, 128), heap(OEYE, 128),
           start=True, stop=True, waits=[("sV", nv), aot_rel[p]], inc=True)
        ao_mm = B.val("sT")
        ascr_free[p] = ("sT", ao_mm)
        aov = act(heap(OX + 128 * g, 128), BNK[:, 384:512], AF.Identity,
                  waits=[("sT", ao_mm), x_dead])
        aot_rel[p] = ("sS", aov)
        b_rel[p] = ("sS", aov)
        ao_done = ("sS", aov)

    att_T = ("sT", B.val("sT"))
    att_V = ("sV", B.val("sV"))
    prel[id(P0)] = kgc_act[1]
    prel[id(P1)] = vgc_act[1]
    prel[id(P2)] = avt_act[1]
    prel[id(P3)] = aot_rel[0]
    prel[id(P4)] = aot_rel[1]
    if "dbg_ao" in dbg_aps:
        dma("sync", dbg_aps["dbg_ao"], heap(OX, NIC * T), waits=[ao_done],
            sem="sDBG")
    if stop_after == "attn":
        return _finish(nc, B, sems)

    h1f = heapf(OH, NIC * T)
    dma("sync", h1f.rearrange("p (ic t) -> p ic t", t=T),
        xRT.rearrange("(ic p) t -> p ic t", p=128),
        waits=[att_T, att_V, ao_done, ct_done, st_done,
               stg_state["rel"][0], stg_state["rel"][1]], sem="sX")
    xres_loaded = ("sX", B.val("sX"))
    stg_state["base"] = OK
    wring["base"] = OK + 4096
    wring["rel"] = [att_T, att_T, att_T]
    stg_state["rel"] = [att_T, att_T]

    ao3 = heap(OX, NIC * T).rearrange("p (u e h) -> p e h u", e=16, h=16)
    h1_done = []
    for oc in range(NOC):
        slab, wmk, slot = stream_weight(w_in["wo"][oc, :, :], "vector")
        P = pick_big()
        for ec in range(NIC):
            for hs in range(2):
                mm(P[:, hs * 512:(hs + 1) * 512],
                   slab[:, ec * 128:(ec + 1) * 128],
                   ao3[:, ec, hs * 8:(hs + 1) * 8, :],
                   start=(ec == 0), stop=(ec == NIC - 1),
                   waits=[wmk, prel[id(P)], ao_done]
                   if (ec == 0 and hs == 0) else (),
                   inc=(ec == NIC - 1 and hs == 1))
        mmv = B.val("sT")
        wring["rel"][slot] = ("sT", mmv)
        rv = vop("scalar_tensor_tensor", h1f[:, oc * T:(oc + 1) * T], P[:],
                 bias_t[:, 48 + oc:48 + oc + 1], h1f[:, oc * T:(oc + 1) * T],
                 ALU.add, ALU.add, waits=[("sT", mmv), xres_loaded])
        release(P, "sV", rv)
        h1_done.append(("sV", rv))

    if "dbg_h1" in dbg_aps:
        dma("sync", dbg_aps["dbg_h1"], h1f, waits=[h1_done[-1]], sem="sDBG")

    OLN = OQ

    def layernorm(src_done, bf_dst_base, src_guard=None):
        h1bf_pp = [heap(OLN, 1024), heap(OLN + 1024, 1024)]
        h1sq_pp = [heap(OLN + 2048, 1024), heap(OLN + 3072, 1024)]
        pp_rel = [src_guard, src_guard]
        for ic in range(NIC):
            pp = ic % 2
            sv = act(h1bf_pp[pp], h1f[:, ic * T:(ic + 1) * T], AF.Identity,
                     waits=[src_done[ic], pp_rel[pp]])
            qv = vop("tensor_tensor", h1sq_pp[pp], h1bf_pp[pp], h1bf_pp[pp],
                     ALU.mult, waits=[("sS", sv)])
            for half in range(2):
                PT = P3 if half == 0 else P4
                mm(PT[0:1, 0:512], ones1_t[:],
                   h1bf_pp[pp][:, half * 512:(half + 1) * 512],
                   start=(ic == 0), stop=(ic == NIC - 1),
                   waits=[("sS", sv), prel[id(PT)], ("sV", memset_done)]
                   if ic == 0 else [("sS", sv)])
            for half in range(2):
                mm(P0[0:1, half * 512:(half + 1) * 512], ones1_t[:],
                   h1sq_pp[pp][:, half * 512:(half + 1) * 512],
                   start=(ic == 0), stop=(ic == NIC - 1),
                   waits=[("sV", qv), prel[id(P0)]] if ic == 0 else [("sV", qv)],
                   inc=(half == 1))
            pp_rel[pp] = ("sT", B.val("sT"))
        ln_mm = B.val("sT")
        sc = 1.0 / float(E)
        vop("tensor_scalar_mul", stats_t[0:1, 0:512], P3[0:1, 0:512], sc,
            waits=[("sT", ln_mm)])
        vop("tensor_scalar_mul", stats_t[0:1, 512:1024], P4[0:1, 0:512], sc)
        msqv = vop("tensor_scalar_mul", stats_t[0:1, 1024:2048],
                   P0[0:1, 0:1024], sc)
        release(P0, "sV", msqv)
        vop("tensor_tensor", vscr_t[0:1, 0:512], stats_t[0:1, 0:512],
            stats_t[0:1, 0:512], ALU.mult)
        vop("tensor_tensor", stats_t[0:1, 1024:1536], stats_t[0:1, 1024:1536],
            vscr_t[0:1, 0:512], ALU.subtract)
        vop("tensor_tensor", vscr_t[0:1, 0:512], stats_t[0:1, 512:1024],
            stats_t[0:1, 512:1024], ALU.mult)
        vv = vop("tensor_tensor", stats_t[0:1, 1536:2048], stats_t[0:1, 1536:2048],
                 vscr_t[0:1, 0:512], ALU.subtract)
        release(P3, "sV", vv)
        release(P4, "sV", vv)
        sq = act(stats_t[0:1, 1024:2048], stats_t[0:1, 1024:2048], AF.Sqrt,
                 bias=eps_ap, waits=[("sV", vv), ("sV", memset_done)])
        rstdv = vop("reciprocal", stats_t[0:1, 1024:2048], stats_t[0:1, 1024:2048],
                    waits=[("sS", sq)])
        mrv = vop("tensor_tensor", stats_t[0:1, 0:1024], stats_t[0:1, 0:1024],
                  stats_t[0:1, 1024:2048], ALU.mult)
        for half in range(2):
            mm(P1[:, half * 512:(half + 1) * 512], onesr_t[:],
               stats_t[0:1, 1024 + half * 512:1024 + (half + 1) * 512],
               start=True, stop=True,
               waits=[("sV", rstdv), prel[id(P1)]] if half == 0 else ())
        for half in range(2):
            mm(P2[:, half * 512:(half + 1) * 512], onesr_t[:],
               stats_t[0:1, half * 512:(half + 1) * 512],
               start=True, stop=True,
               waits=[("sV", mrv), prel[id(P2)]] if half == 0 else (),
               inc=(half == 1))
        bc = B.val("sT")
        out_done = []
        bf_done = []
        for ic in range(NIC):
            for half in range(2):
                vop("tensor_tensor", vscr_t[:],
                    h1f[:, ic * T + half * 512:ic * T + (half + 1) * 512],
                    P1[:, half * 512:(half + 1) * 512], ALU.mult,
                    waits=[("sT", bc)])
                nv = vop("tensor_tensor",
                         h1f[:, ic * T + half * 512:ic * T + (half + 1) * 512],
                         vscr_t[:], P2[:, half * 512:(half + 1) * 512],
                         ALU.subtract)
            out_done.append(("sV", nv))
            if bf_dst_base is not None:
                bv = act(heap(bf_dst_base + ic * T, T),
                         h1f[:, ic * T:(ic + 1) * T], AF.Identity,
                         waits=[("sV", nv)])
                bf_done.append(("sS", bv))
        lastv = B.val("sV")
        release(P1, "sV", lastv)
        release(P2, "sV", lastv)
        return out_done, bf_done, lastv

    ln1_done, h1nbf_done, ln1_v = layernorm(h1_done, OX, src_guard=ao_done)
    if "dbg_ln1" in dbg_aps:
        dma("sync", dbg_aps["dbg_ln1"], h1f, waits=[("sV", ln1_v)], sem="sDBG")

    if stop_after == "ln1":
        return _finish(nc, B, sems)

    OGATED = OQ + 4096
    OSG = OQ + 12288
    wdring_rel = [att_T] * HB
    sg_pp = [heap(OSG, 1024), heap(OSG + 1024, 1024)]
    gated = [heap(OGATED + j * 1024, 1024) for j in range(HB)]
    gated_rel = [None] * HB
    sg_rel = [None, None]
    stream_done = [None] * NIC
    ln_last = ("sV", ln1_v)

    for b in range(NBLK):
        hc0 = b * HB
        gdone = [None] * HB
        for j in range(HB):
            hc = hc0 + j
            slabg, wmkg, slotg = stream_weight(w_in["wg"][hc, :, :], "vector")
            Pg = pick_big()
            for ic in range(NIC):
                for half in range(2):
                    mm(Pg[:, half * 512:(half + 1) * 512],
                       slabg[:, ic * 128:(ic + 1) * 128],
                       heap(OX + ic * T + half * 512, 512),
                       start=(ic == 0), stop=(ic == NIC - 1),
                       waits=[wmkg, prel[id(Pg)], h1nbf_done[NIC - 1]]
                       if (ic == 0 and half == 0) else (),
                       inc=(ic == NIC - 1 and half == 1))
            gv = B.val("sT")
            wring["rel"][slotg] = ("sT", gv)
            slabu, wmku, slotu = stream_weight(w_in["wu"][hc, :, :], "vector")
            Pu = pick_big()
            for ic in range(NIC):
                for half in range(2):
                    mm(Pu[:, half * 512:(half + 1) * 512],
                       slabu[:, ic * 128:(ic + 1) * 128],
                       heap(OX + ic * T + half * 512, 512),
                       start=(ic == 0), stop=(ic == NIC - 1),
                       waits=[wmku, prel[id(Pu)]] if (ic == 0 and half == 0)
                       else (),
                       inc=(ic == NIC - 1 and half == 1))
            uv = B.val("sT")
            wring["rel"][slotu] = ("sT", uv)
            pp = j % 2
            sgv = act(sg_pp[pp], Pg[:], AF.Silu,
                      waits=[("sT", gv), sg_rel[pp], ln_last])
            release(Pg, "sS", sgv)
            gtv = vop("tensor_tensor", gated[j], sg_pp[pp], Pu[:], ALU.mult,
                      waits=[("sS", sgv), ("sT", uv), gated_rel[j], ln_last])
            sg_rel[pp] = ("sV", gtv)
            release(Pu, "sV", gtv)
            gdone[j] = ("sV", gtv)
        wd_mk = [None] * HB
        for j in range(HB):
            hc = hc0 + j
            wd_mk[j] = load_cast_slab(w_in["wd"][hc, :, :], heap(OV + j * E, E),
                                      wdring_rel[j], "vector")
        for oc in range(NOC):
            P = pick_big()
            for half in range(2):
                for j in range(HB):
                    first = half == 0 and j == 0
                    w = ([prel[id(P)]] + [wd_mk[jj] for jj in range(HB)]
                         + list(gdone)) if first else ()
                    mm(P[:, half * 512:(half + 1) * 512],
                       heap(OV + j * E, E)[:, oc * 128:(oc + 1) * 128],
                       gated[j][:, half * 512:(half + 1) * 512],
                       start=(j == 0), stop=(j == HB - 1),
                       waits=w, inc=(half == 1 and j == HB - 1))
            dmv = B.val("sT")
            av = vop("tensor_tensor", h1f[:, oc * T:(oc + 1) * T],
                     h1f[:, oc * T:(oc + 1) * T], P[:], ALU.add,
                     waits=[("sT", dmv), ln1_done[oc]])
            release(P, "sV", av)
            if b == NBLK - 1:
                stream_done[oc] = ("sV", av)
        dmv_all = ("sT", B.val("sT"))
        for j in range(HB):
            wdring_rel[j] = dmv_all
            gated_rel[j] = dmv_all

    ln2_done, _, ln2_v = layernorm(stream_done, None)
    dma("sync", outT.rearrange("(ic p) t -> p ic t", p=128),
        h1f.rearrange("p (ic t) -> p ic t", t=T),
        waits=[("sV", ln2_v)], sem="sST")
    return _finish(nc, B, sems)


def _finish(nc, B, sems):
    fin = [(nm, B.val(nm)) for nm in sems if B.val(nm) > 0
           and nm not in ("sT", "sS", "sV")]

    def f(e, fin=fin):
        for nm, v in fin:
            e.wait_ge(sems[nm], v)
    B.ops["sync"].append((f, (), None))

    with nc.Block() as block:
        for eng in ("tensor", "vector", "scalar", "sync", "gpsimd"):
            ops = B.ops[eng]
            if not ops:
                continue

            def body(e, ops=ops):
                for fn, waits, inc in ops:
                    for sname, v in waits:
                        e.wait_ge(sems[sname], v)
                    r = fn(e)
                    if inc is not None:
                        r.then_inc(sems[inc[0]], inc[1])
            getattr(block, eng)(body)
    return nc


_NC_CACHE = {}


def _get_nc(debug=frozenset(), stop_after="full"):
    key = (debug, stop_after)
    if key not in _NC_CACHE:
        _NC_CACHE[key] = build_nc(debug, stop_after)
    return _NC_CACHE[key]


def _out_rows(pc):
    return np.array([256 * h + 64 * pc + u for h in range(16) for u in range(64)])


def prep_inputs(inputs):
    x = np.asarray(inputs["L0_attn_x"], dtype=np.float32)
    shared = {
        "wq": _tile_lhsT(np.asarray(inputs["L0_attn_Wq"], np.float32), E, E),
        "wk": _tile_lhsT(np.asarray(inputs["L0_attn_Wk"], np.float32), E, E),
        "wv": _tile_lhsT(np.asarray(inputs["L0_attn_Wv"], np.float32), E, E),
        "wo": _tile_lhsT(np.asarray(inputs["L0_attn_Wo"], np.float32), E, E),
        "wg": _tile_lhsT(np.asarray(inputs["L0_ffn_w_gate"], np.float32), HID, E),
        "wu": _tile_lhsT(np.asarray(inputs["L0_ffn_w_up"], np.float32), HID, E),
        "wd": _tile_lhsT_by_in(np.asarray(inputs["L0_ffn_w_down"], np.float32),
                               E, HID),
        "eye": np.eye(128, dtype=BF),
        "mask": np.kron(np.eye(SP), np.ones((16, 16))).astype(BF),
    }
    bias = np.zeros((128, 64), np.float32)
    bias[:, 0:16] = np.asarray(inputs["L0_attn_bq"], np.float32).reshape(16, 128).T
    bias[:, 16:32] = np.asarray(inputs["L0_attn_bk"], np.float32).reshape(16, 128).T
    bias[:, 32:48] = np.asarray(inputs["L0_attn_bv"], np.float32).reshape(16, 128).T
    bias[:, 48:64] = np.asarray(inputs["L0_attn_bo"], np.float32).reshape(16, 128).T
    shared["bias"] = bias

    in_maps = []
    for c in range(8):
        b, pc = c // 4, c % 4
        pos = np.arange(1024 * pc, 1024 * (pc + 1))
        ct, st = _rope_tables_for(pos)
        m = {"xQT": np.ascontiguousarray(x[b, pos, :].T),
             "xRT": np.ascontiguousarray(x[b, _out_rows(pc), :].T),
             "ropec": np.ascontiguousarray(ct),
             "ropes": np.ascontiguousarray(st)}
        m.update(shared)
        in_maps.append(m)
    return in_maps


def kernel(**inputs):
    in_maps = prep_inputs(inputs)
    nc = _get_nc()
    res = run_bass_kernel_spmd(nc, in_maps, core_ids=list(range(8)))
    out = np.empty((2, 4096, E), dtype=np.float32)
    for c in range(8):
        b, pc = c // 4, c % 4
        out[b, _out_rows(pc), :] = np.asarray(res.results[c]["outT"]).T
    return out
